# version 10
# baseline (speedup 1.0000x reference)
import numpy as np
import ml_dtypes
import concourse.bass as bass
import concourse.mybir as mybir
from concourse.bass_utils import run_bass_kernel_spmd
from contextlib import ExitStack

F32 = mybir.dt.float32
BF16 = mybir.dt.bfloat16
F32R = mybir.dt.float32r
I32 = mybir.dt.int32
Act = mybir.ActivationFunctionType
Alu = mybir.AluOpType
AX = mybir.AxisListType.X

NCORE = 8
T = 4096
C = 1024
HID = 3072
G = 8
HG = 24
TC = 512
NCH = T // TC
D = 128
P_INIT = -1.0e30


def build_nc():
    nc = bass.Bass("TRN2", target_bir_lowering=False)

    x_hbm = nc.dram_tensor("x_cm", [C, T], F32, kind="ExternalInput").ap()
    wk_hbm = nc.dram_tensor("wkT", [C, C], F32, kind="ExternalInput").ap()
    wr_hbm = nc.dram_tensor("wrT", [C, C], F32, kind="ExternalInput").ap()
    wo_hbm = nc.dram_tensor("woT", [C, C], F32, kind="ExternalInput").ap()
    td_hbm = nc.dram_tensor("td_in", [D, G], F32, kind="ExternalInput").ap()
    tf_hbm = nc.dram_tensor("tf_in", [D, G], F32, kind="ExternalInput").ap()
    wg_hbm = nc.dram_tensor("wgT", [C, HID], I32, kind="ExternalInput").ap()
    wu_hbm = nc.dram_tensor("wuT", [C, HID], I32, kind="ExternalInput").ap()
    wd_hbm = nc.dram_tensor("wdT", [HID, C], I32, kind="ExternalInput").ap()
    out_hbm = nc.dram_tensor("out_cm", [C, T], F32, kind="ExternalOutput").ap()
    xn_scr = nc.dram_tensor("xn_scr", [C, T], BF16, kind="Internal").ap()
    wg_bfd = nc.dram_tensor("wg_bfd", [C, HID], BF16, kind="Internal").ap()
    wu_bfd = nc.dram_tensor("wu_bfd", [C, HID], BF16, kind="Internal").ap()
    wd_bfd = nc.dram_tensor("wd_bfd", [HID, C], BF16, kind="Internal").ap()
    pt_scr = nc.dram_tensor("pt_scr", [C, T], F32, kind="Internal").ap()

    def hbm_chunk(ap_, j, dt_cols=TC):
        return ap_[:, j * dt_cols:(j + 1) * dt_cols].rearrange("(g p) t -> p g t", p=D)

    es = ExitStack()
    with es:
        sem = {}
        for name in [
            "sPio", "g_wk", "g_wr", "g_wo", "g_fc", "s_pexp", "v_pw", "s_pew",
            "sA_x0", "sA_x1", "sA_x2", "v_sq", "t_stats", "v_var", "s_srow", "v_sv",
            "t_bcs", "s_sev", "t_bcm", "s_msev", "v_xn", "g_xn0", "g_xn1",
            "t_kog", "t_rog", "s_kev", "s_rev", "v_m", "s_es",
            "s_act", "v_den", "s_rec", "v_out", "t_wo", "v_res", "g_part0", "g_part1",
            "ph",
            "g_wgb", "g_wub", "g_wdb", "g_wgb2", "g_wub2", "g_wdb2", "sB_xn", "sB_pb", "t_g", "t_u", "s_sil",
            "v_hact", "t_p1", "v_res1", "t_p2", "v_res2", "gB_out",
        ]:
            sem[name] = es.enter_context(nc.semaphore(name))

        def sa_x(j):
            return sem[f"sA_x{j % 3}"], 16 * (j // 3 + 1)

        def g_xn_done(j):
            return sem[f"g_xn{j % 2}"], 16 * (j // 2 + 1)

        def g_part_done(j):
            return sem[f"g_part{j % 2}"], 16 * (j // 2 + 1)

        sA = ExitStack()
        tdt = sA.enter_context(nc.sbuf_tensor("tdt", [D, G], F32)).ap()
        tft = sA.enter_context(nc.sbuf_tensor("tft", [D, G], F32)).ap()
        exptd = sA.enter_context(nc.sbuf_tensor("exptd", [D, G], F32)).ap()
        wv = sA.enter_context(nc.sbuf_tensor("wv", [D, G], F32)).ap()
        ewv = sA.enter_context(nc.sbuf_tensor("ewv", [D, G], F32)).ap()
        euv = sA.enter_context(nc.sbuf_tensor("euv", [D, G], F32)).ap()
        relu_u = sA.enter_context(nc.sbuf_tensor("relu_u", [D, G], F32)).ap()
        a_all = sA.enter_context(nc.sbuf_tensor("a_all", [D, G], F32)).ap()
        b_all = sA.enter_context(nc.sbuf_tensor("b_all", [D, G], F32)).ap()
        Mall = sA.enter_context(nc.sbuf_tensor("Mall", [D, G], F32)).ap()
        t1b = sA.enter_context(nc.sbuf_tensor("t1b", [D, G], F32)).ap()
        m_all2 = [sA.enter_context(nc.sbuf_tensor(f"m_all{i}", [D, G], F32)).ap() for i in range(2)]
        neg_m2 = [sA.enter_context(nc.sbuf_tensor(f"neg_m{i}", [D, G], F32)).ap() for i in range(2)]
        dp2 = [sA.enter_context(nc.sbuf_tensor(f"dp{i}", [D, G], F32)).ap() for i in range(2)]
        esb2 = [sA.enter_context(nc.sbuf_tensor(f"esb{i}", [D, G], F32)).ap() for i in range(2)]
        a02 = [sA.enter_context(nc.sbuf_tensor(f"a0{i}", [D, G], F32)).ap() for i in range(2)]
        b02 = [sA.enter_context(nc.sbuf_tensor(f"b0{i}", [D, G], F32)).ap() for i in range(2)]
        onescol_bf = sA.enter_context(nc.sbuf_tensor("onescol_bf", [D, 1], BF16)).ap()
        onescol_f = sA.enter_context(nc.sbuf_tensor("onescol_f", [D, 1], F32)).ap()
        eps_t = sA.enter_context(nc.sbuf_tensor("eps_t", [D, 1], F32)).ap()
        nc.const_aps.register(F32, 1e-5, eps_t)
        ones1 = sA.enter_context(nc.sbuf_tensor("ones1", [1, D], F32)).ap()
        ew_full = sA.enter_context(nc.sbuf_tensor("ew_full", [D, G * TC], F32)).ap()
        wk_bf = sA.enter_context(nc.sbuf_tensor("wk_bf", [D, G * 1024], BF16)).ap()
        wr_bf = sA.enter_context(nc.sbuf_tensor("wr_bf", [D, G * 1024], BF16)).ap()
        wo_bf = sA.enter_context(nc.sbuf_tensor("wo_bf", [D, G * 1024], BF16)).ap()
        x_buf = [sA.enter_context(nc.sbuf_tensor(f"x_buf{i}", [D, G * TC], F32)).ap()
                 for i in range(3)]
        xn_buf = [sA.enter_context(nc.sbuf_tensor(f"xn_buf{i}", [D, G * TC], BF16)).ap()
                  for i in range(2)]
        k_sb = sA.enter_context(nc.sbuf_tensor("k_sb", [D, G * TC], F32)).ap()
        r_sb = sA.enter_context(nc.sbuf_tensor("r_sb", [D, G * TC], F32)).ap()
        outpre = sA.enter_context(nc.sbuf_tensor("outpre", [D, G * TC], BF16)).ap()
        rows = sA.enter_context(nc.sbuf_tensor("rows", [1, 3 * TC], F32)).ap()
        m_row = rows[0:1, 0:TC]
        msq_row = rows[0:1, TC:2 * TC]
        s_row = rows[0:1, 2 * TC:3 * TC]
        ms_row = m_row
        var_row = s_row
        sb_sb = sA.enter_context(nc.sbuf_tensor("sb_sb", [D, TC], F32)).ap()
        msb_sb = sA.enter_context(nc.sbuf_tensor("msb_sb", [D, TC], F32)).ap()
        ek_s = [sA.enter_context(nc.sbuf_tensor(f"ek{i}", [D, TC], F32)).ap() for i in range(3)]
        eu_1 = sA.enter_context(nc.sbuf_tensor("eu_1", [D, TC], F32)).ap()
        enr_s = [sA.enter_context(nc.sbuf_tensor(f"enr{i}", [D, TC], F32)).ap() for i in range(3)]
        zA_s = [sA.enter_context(nc.sbuf_tensor(f"zA{i}", [D, TC], F32)).ap() for i in range(2)]
        den_1 = sA.enter_context(nc.sbuf_tensor("den_1", [D, TC], F32)).ap()
        SA_e = [sA.enter_context(nc.sbuf_tensor(f"SA{i}", [D, TC + 4], F32)).ap() for i in range(2)]
        SB_e = [sA.enter_context(nc.sbuf_tensor(f"SB{i}", [D, TC + 4], F32)).ap() for i in range(2)]
        psA_st = sA.enter_context(nc.psum_tensor("psA_st", [64, TC], F32)).ap()
        psA_bc = sA.enter_context(nc.psum_tensor("psA_bc", [D, TC], F32)).ap()
        ps_k = sA.enter_context(nc.psum_tensor("ps_k", [D, 2 * TC], F32)).ap()
        ps_r = sA.enter_context(nc.psum_tensor("ps_r", [D, 2 * TC], F32)).ap()
        ps_wo = sA.enter_context(nc.psum_tensor("ps_wo", [D, 2 * TC], F32)).ap()

        def gsl(ap_, g, w=TC):
            return ap_[:, g * w:(g + 1) * w]

        with nc.Block() as blk:

            @blk.sync
            def _(sy):
                sy.dma_start(out=tdt, in_=td_hbm).then_inc(sem["sPio"], 16)
                sy.dma_start(out=tft, in_=tf_hbm).then_inc(sem["sPio"], 16)
                for jj in range(3):
                    sy.dma_start(
                        out=x_buf[jj].rearrange("p (g t) -> p g t", t=TC),
                        in_=hbm_chunk(x_hbm, jj)).then_inc(sem[f"sA_x{jj % 3}"], 16)
                for jj in range(3, NCH):
                    sy.wait_ge(*g_part_done(jj - 3))
                    sy.dma_start(
                        out=x_buf[jj % 3].rearrange("p (g t) -> p g t", t=TC),
                        in_=hbm_chunk(x_hbm, jj)).then_inc(sem[f"sA_x{jj % 3}"], 16)

            @blk.scalar
            def _(sc):
                sc.wait_ge(sem["sPio"], 32)
                sc.activation(out=exptd, in_=tdt, func=Act.Exp).then_inc(sem["s_pexp"], 1)
                sc.wait_ge(sem["v_pw"], 1)
                sc.activation(out=ewv, in_=wv, func=Act.Exp)
                sc.activation(out=euv, in_=tft,
                              func=Act.Exp).then_inc(sem["s_pew"], 1)
                sc.wait_ge(sem["v_m"], 1)
                sc.activation(out=esb2[0], in_=dp2[0], func=Act.Exp).then_inc(sem["s_es"], 1)
                sc.wait_ge(sem["v_var"], 1)
                sc.activation(out=msq_row, in_=var_row, func=Act.Ln, bias=1e-5)
                sc.drain()
                sc.activation(out=s_row, in_=msq_row, func=Act.Exp,
                              scale=-0.5).then_inc(sem["s_srow"], 1)
                sc.wait_ge(sem["t_bcs"], 1)
                sc.copy(out=sb_sb, in_=psA_bc).then_inc(sem["s_sev"], 1)
                sc.wait_ge(sem["t_bcm"], 1)
                sc.copy(out=msb_sb, in_=psA_bc).then_inc(sem["s_msev"], 1)

                for j in range(NCH):
                    q = j % 2
                    for og in range(G):
                        idx = 8 * j + og
                        if j >= 1:
                            sc.wait_ge(sem["v_den"], 8 * (j - 1) + og + 1)
                        sc.wait_ge(sem["t_kog"], idx + 1)
                        sc.copy(out=gsl(k_sb, og), in_=gsl(ps_k, og % 2)).then_inc(sem["s_kev"], 1)
                        sc.wait_ge(sem["t_rog"], idx + 1)
                        sc.copy(out=gsl(r_sb, og), in_=gsl(ps_r, og % 2)).then_inc(sem["s_rev"], 1)
                        if idx >= 3:
                            sc.wait_ge(sem["v_out"], idx - 2)
                            sc.wait_ge(sem["v_den"], idx - 2)
                        sc.drain()
                        sc.activation(out=ek_s[idx % 3], in_=gsl(k_sb, og), func=Act.Exp,
                                      bias=neg_m2[q][:, og:og + 1])
                        sc.activation(out=enr_s[idx % 3], in_=gsl(r_sb, og), func=Act.Exp,
                                      scale=-1.0).then_inc(sem["s_act"], 1)
                        if og >= 1:
                            pidx = idx - 1
                            sc.wait_ge(sem["v_den"], pidx + 1)
                            sc.activation(out=eu_1, in_=den_1, func=Act.Ln)
                            sc.drain()
                            sc.activation(out=ek_s[(idx - 1) % 3], in_=eu_1, func=Act.Exp,
                                          scale=-1.0).then_inc(sem["s_rec"], 1)
                    if j + 1 < NCH:
                        sc.wait_ge(sem["v_var"], j + 2)
                        sc.activation(out=msq_row, in_=var_row, func=Act.Ln, bias=1e-5)
                        sc.drain()
                        sc.activation(out=s_row, in_=msq_row, func=Act.Exp,
                                      scale=-0.5).then_inc(sem["s_srow"], 1)
                    pidx = 8 * j + 7
                    sc.wait_ge(sem["v_den"], pidx + 1)
                    sc.activation(out=eu_1, in_=den_1, func=Act.Ln)
                    sc.drain()
                    sc.activation(out=ek_s[pidx % 3], in_=eu_1, func=Act.Exp,
                                  scale=-1.0).then_inc(sem["s_rec"], 1)
                    if j + 1 < NCH:
                        q1 = (j + 1) % 2
                        sc.wait_ge(sem["v_m"], j + 2)
                        sc.activation(out=esb2[q1], in_=dp2[q1],
                                      func=Act.Exp).then_inc(sem["s_es"], 1)
                        sc.wait_ge(sem["t_bcs"], j + 2)
                        sc.copy(out=sb_sb, in_=psA_bc).then_inc(sem["s_sev"], 1)
                        sc.wait_ge(sem["t_bcm"], j + 2)
                        sc.copy(out=msb_sb, in_=psA_bc).then_inc(sem["s_msev"], 1)

            @blk.vector
            def _(ve):
                ve.memset(onescol_bf, 1.0)
                ve.memset(onescol_f, 1.0)
                ve.memset(eps_t, 1e-5)
                ve.memset(ones1, 1.0)
                ve.memset(a_all, 0.0)
                ve.memset(b_all, 0.0)
                ve.memset(Mall, 0.0)
                ve.memset(m_all2[1], P_INIT)
                ve.memset(den_1, 1.0)
                ve.drain()
                ve.wait_ge(sem["s_pexp"], 1)
                ve.tensor_scalar_mul(wv, exptd, -1.0).then_inc(sem["v_pw"], 1)
                ve.drain()
                ve.wait_ge(sem["s_pew"], 1)
                for g in range(G):
                    ve.tensor_scalar_mul(gsl(ew_full, g), den_1, ewv[:, g:g + 1])
                ve.wait_ge(sem["sPio"], 32)
                ve.tensor_scalar_max(relu_u, tft, 0.0)
                ve.drain()
                ve.tensor_add(t1b, Mall, relu_u)
                ve.drain()
                ve.tensor_max(m_all2[0], t1b, m_all2[1])
                ve.drain()
                ve.tensor_scalar_mul(neg_m2[0], m_all2[0], -1.0)
                ve.tensor_sub(dp2[0], m_all2[1], m_all2[0]).then_inc(sem["v_m"], 1)
                ve.wait_ge(*sa_x(0))
                ve.tensor_mul(xn_buf[0], x_buf[0], x_buf[0]).then_inc(sem["v_sq"], 1)
                ve.wait_ge(*sa_x(1))
                ve.tensor_mul(xn_buf[1], x_buf[1], x_buf[1]).then_inc(sem["v_sq"], 1)
                ve.wait_ge(sem["s_es"], 1)
                ve.tensor_mul(a02[0], a_all, esb2[0])
                ve.tensor_mul(b02[0], b_all, esb2[0])
                ve.wait_ge(sem["t_stats"], 1)
                ve.tensor_scalar_mul(m_row, psA_st[0:1, :], 1.0 / C)
                ve.tensor_scalar_mul(var_row, psA_st[32:33, :], 1.0 / C)
                ve.drain()
                ve.tensor_mul(msq_row, m_row, m_row)
                ve.drain()
                ve.scalar_tensor_tensor(
                    out=var_row, in0=msq_row, scalar=-1.0, in1=var_row,
                    op0=Alu.mult, op1=Alu.add).then_inc(sem["v_var"], 1)
                ve.wait_ge(sem["s_srow"], 1)
                ve.tensor_mul(ms_row, m_row, s_row).then_inc(sem["v_sv"], 1)
                ve.wait_ge(sem["s_msev"], 1)
                xtmp = [zA_s[0], zA_s[1], den_1, eu_1,
                        SA_e[0][:, 0:TC], SA_e[1][:, 0:TC],
                        SB_e[0][:, 0:TC], SB_e[1][:, 0:TC]]
                for g in range(G):
                    ve.tensor_mul(xtmp[g], gsl(x_buf[0], g), sb_sb)
                ve.drain()
                for g in range(G - 1):
                    ve.tensor_sub(gsl(xn_buf[0], g), xtmp[g], msb_sb)
                ve.tensor_sub(gsl(xn_buf[0], G - 1), xtmp[G - 1],
                              msb_sb).then_inc(sem["v_xn"], 1)

                for j in range(NCH):
                    q = j % 2
                    q1 = (j + 1) % 2
                    pj3 = j % 3

                    def outpre_g(g):
                        idx = 8 * j + g
                        pg = g % 2
                        if g == 0:
                            ve.wait_ge(sem["t_wo"], 8 * j)
                        ve.wait_ge(sem["s_rec"], idx + 1)
                        ve.tensor_mul(gsl(outpre, g), zA_s[pg],
                                      ek_s[idx % 3]).then_inc(sem["v_out"], 1)

                    def scan_group(g):
                        idx = 8 * j + g
                        pg = g % 2
                        ve.wait_ge(sem["s_act"], idx + 1)
                        if g >= 2:
                            outpre_g(g - 2)
                        ve.tensor_copy(SA_e[pg][:, 0:1], a02[q][:, g:g + 1])
                        ve.tensor_copy(SB_e[pg][:, 0:1], b02[q][:, g:g + 1])
                        ve.drain()
                        ve.tensor_mul(zA_s[pg], ek_s[idx % 3], gsl(r_sb, g))
                        ve.drain()
                        ve.tensor_tensor_scan(
                            out=SA_e[pg][:, 1:TC + 1], data0=gsl(ew_full, g),
                            data1=zA_s[pg], initial=a02[q][:, g:g + 1],
                            op0=Alu.mult, op1=Alu.add)
                        ve.tensor_tensor_scan(
                            out=SB_e[pg][:, 1:TC + 1], data0=gsl(ew_full, g),
                            data1=ek_s[idx % 3], initial=b02[q][:, g:g + 1],
                            op0=Alu.mult, op1=Alu.add)
                        ve.drain()
                        ve.scalar_tensor_tensor(
                            out=zA_s[pg], in0=zA_s[pg], scalar=euv[:, g:g + 1],
                            in1=SA_e[pg][:, 0:TC], op0=Alu.mult, op1=Alu.add)
                        if idx >= 1:
                            ve.wait_ge(sem["s_rec"], idx)
                        ve.scalar_tensor_tensor(
                            out=den_1, in0=ek_s[idx % 3], scalar=euv[:, g:g + 1],
                            in1=SB_e[pg][:, 0:TC], op0=Alu.mult, op1=Alu.add)
                        ve.tensor_copy(a_all[:, g:g + 1], SA_e[pg][:, TC:TC + 1])
                        ve.tensor_copy(b_all[:, g:g + 1], SB_e[pg][:, TC:TC + 1])
                        ve.drain()
                        ve.scalar_tensor_tensor(
                            out=den_1, in0=enr_s[idx % 3], scalar=1.0,
                            in1=den_1, op0=Alu.add,
                            op1=Alu.mult).then_inc(sem["v_den"], 1)

                    def resid_pair(i):
                        for cg in (2 * i, 2 * i + 1):
                            ve.wait_ge(sem["t_wo"], 8 * (j - 1) + cg + 1)
                            ve.tensor_add(gsl(x_buf[(j - 1) % 3], cg),
                                          gsl(x_buf[(j - 1) % 3], cg),
                                          gsl(ps_wo, cg % 2)).then_inc(sem["v_res"], 1)
                        ve.drain()

                    for g in range(6):
                        if j >= 1 and g < 4:
                            resid_pair(g)
                        scan_group(g)
                    if j + 1 < NCH:
                        ve.wait_ge(sem["t_stats"], j + 2)
                        ve.tensor_scalar_mul(m_row, psA_st[0:1, :], 1.0 / C)
                        ve.tensor_scalar_mul(var_row, psA_st[32:33, :], 1.0 / C)
                        ve.drain()
                        ve.tensor_mul(msq_row, m_row, m_row)
                        ve.drain()
                        ve.scalar_tensor_tensor(
                            out=var_row, in0=msq_row, scalar=-1.0, in1=var_row,
                            op0=Alu.mult, op1=Alu.add).then_inc(sem["v_var"], 1)
                    scan_group(6)
                    if j + 1 < NCH:
                        ve.wait_ge(sem["s_srow"], j + 2)
                        ve.tensor_mul(ms_row, m_row, s_row).then_inc(sem["v_sv"], 1)
                    scan_group(7)
                    outpre_g(G - 2)
                    outpre_g(G - 1)
                    ve.drain()
                    for og in range(G):
                        ve.wait_ge(sem["s_kev"], 8 * j + og + 1)
                        ve.reduce_max(Mall[:, og:og + 1], gsl(k_sb, og), axis=AX)
                    if j + 1 < NCH:
                        ve.drain()
                        ve.tensor_add(t1b, Mall, relu_u)
                        ve.drain()
                        ve.tensor_max(m_all2[q1], t1b, m_all2[q])
                        ve.drain()
                        ve.tensor_scalar_mul(neg_m2[q1], m_all2[q1], -1.0)
                        ve.tensor_sub(dp2[q1], m_all2[q], m_all2[q1]).then_inc(sem["v_m"], 1)
                        ve.wait_ge(sem["s_es"], j + 2)
                        ve.drain()
                        ve.tensor_mul(a02[q1], a_all, esb2[q1])
                        ve.tensor_mul(b02[q1], b_all, esb2[q1])
                        ve.wait_ge(sem["s_msev"], j + 2)
                        if j >= 1:
                            ve.wait_ge(*g_xn_done(j - 1))
                        ve.drain()
                        for g in range(G):
                            ve.tensor_mul(xtmp[g], gsl(x_buf[(j + 1) % 3], g), sb_sb)
                        ve.drain()
                        for g in range(G - 1):
                            ve.tensor_sub(gsl(xn_buf[q1], g), xtmp[g], msb_sb)
                        ve.tensor_sub(gsl(xn_buf[q1], G - 1), xtmp[G - 1],
                                      msb_sb).then_inc(sem["v_xn"], 1)

                for cg in range(G):
                    ve.wait_ge(sem["t_wo"], 8 * 7 + cg + 1)
                    ve.tensor_add(gsl(x_buf[7 % 3], cg), gsl(x_buf[7 % 3], cg),
                                  gsl(ps_wo, cg % 2)).then_inc(sem["v_res"], 1)

            @blk.tensor
            def _(te):
                def stats(s):
                    te.wait_ge(*sa_x(s))
                    te.wait_ge(sem["v_sq"], s + 1)
                    if s >= 1:
                        te.wait_ge(sem["v_var"], s)
                    for g in range(G):
                        te.matmul(psA_st[0:1, :], onescol_f, gsl(x_buf[s % 3], g),
                                  start=(g == 0), stop=(g == G - 1))
                    for g in range(G):
                        ins = te.matmul(psA_st[32:33, :], onescol_bf, gsl(xn_buf[s % 2], g),
                                        start=(g == 0), stop=(g == G - 1))
                    ins.then_inc(sem["t_stats"], 1)

                def bcasts(s):
                    te.wait_ge(sem["v_sv"], s + 1)
                    if s >= 1:
                        te.wait_ge(sem["s_msev"], s)
                    te.matmul(psA_bc, ones1[0:1, :], s_row,
                              start=True, stop=True).then_inc(sem["t_bcs"], 1)
                    te.wait_ge(sem["s_sev"], s + 1)
                    te.matmul(psA_bc, ones1[0:1, :], ms_row,
                              start=True, stop=True).then_inc(sem["t_bcm"], 1)

                for j in range(NCH):
                    if j == 0:
                        stats(0)
                        bcasts(0)
                    def wo_cg(cg):
                        cgidx = 8 * (j - 1) + cg
                        if cgidx >= 2:
                            te.wait_ge(sem["v_res"], cgidx - 1)
                        for hg in range(G):
                            ins = te.matmul(
                                gsl(ps_wo, cg % 2),
                                wo_bf[:, hg * 1024 + cg * D:hg * 1024 + (cg + 1) * D],
                                gsl(outpre, hg),
                                start=(hg == 0), stop=(hg == G - 1))
                        ins.then_inc(sem["t_wo"], 1)

                    te.wait_ge(sem["v_xn"], j + 1)
                    if j == 0:
                        te.wait_ge(sem["g_wk"], 16)
                        te.wait_ge(sem["g_wr"], 16)
                    def proj_og(og):
                        ogidx = 8 * j + og
                        if ogidx >= 2:
                            te.wait_ge(sem["s_kev"], ogidx - 1)
                        for gk in range(G):
                            ins = te.matmul(
                                gsl(ps_k, og % 2),
                                wk_bf[:, gk * 1024 + og * D:gk * 1024 + (og + 1) * D],
                                gsl(xn_buf[j % 2], gk),
                                start=(gk == 0), stop=(gk == G - 1))
                        ins.then_inc(sem["t_kog"], 1)
                        if ogidx >= 2:
                            te.wait_ge(sem["s_rev"], ogidx - 1)
                        for gk in range(G):
                            ins = te.matmul(
                                gsl(ps_r, og % 2),
                                wr_bf[:, gk * 1024 + og * D:gk * 1024 + (og + 1) * D],
                                gsl(xn_buf[j % 2], gk),
                                start=(gk == 0), stop=(gk == G - 1))
                        ins.then_inc(sem["t_rog"], 1)

                    if j >= 1:
                        te.wait_ge(sem["v_out"], 8 * (j - 1) + 8)
                        if j == 1:
                            te.wait_ge(sem["g_wo"], 16)
                    for og in range(G):
                        if j >= 1 and og < 4:
                            wo_cg(2 * og)
                            wo_cg(2 * og + 1)
                        proj_og(og)
                    if j + 1 < NCH:
                        stats(j + 1)
                        bcasts(j + 1)
                    def wo_cg(cg):
                        cgidx = 8 * (j - 1) + cg
                te.wait_ge(sem["v_out"], 64)
                for cg in range(G):
                    cgidx = 8 * 7 + cg
                    if cgidx >= 2:
                        te.wait_ge(sem["v_res"], cgidx - 1)
                    for hg in range(G):
                        ins = te.matmul(
                            gsl(ps_wo, cg % 2),
                            wo_bf[:, hg * 1024 + cg * D:hg * 1024 + (cg + 1) * D],
                            gsl(outpre, hg),
                            start=(hg == 0), stop=(hg == G - 1))
                    ins.then_inc(sem["t_wo"], 1)

            @blk.gpsimd
            def _(gp):
                gp.wait_ge(sem["sA_x0"], 16)
                gp.dma_start(
                    out=wk_bf.rearrange("p (g n) -> p g n", n=1024),
                    in_=wk_hbm.rearrange("(g p) n -> p g n", p=D)).then_inc(sem["g_wk"], 16)
                gp.dma_start(
                    out=wr_bf.rearrange("p (g n) -> p g n", n=1024),
                    in_=wr_hbm.rearrange("(g p) n -> p g n", p=D)).then_inc(sem["g_wr"], 16)
                gp.dma_start(
                    out=wo_bf.rearrange("p (g n) -> p g n", n=1024),
                    in_=wo_hbm.rearrange("(g p) n -> p g n", p=D)).then_inc(sem["g_wo"], 16)
                fpieces = []
                for r in range(4):
                    fpieces.append((wg_bfd[r * 256:(r + 1) * 256, :],
                                    wg_hbm[r * 256:(r + 1) * 256, :]))
                for r in range(4):
                    fpieces.append((wu_bfd[r * 256:(r + 1) * 256, :],
                                    wu_hbm[r * 256:(r + 1) * 256, :]))
                for r in range(4):
                    fpieces.append((wd_bfd[r * 768:(r + 1) * 768, :],
                                    wd_hbm[r * 768:(r + 1) * 768, :]))
                for j in range(NCH):
                    for pc in range(2 * j, min(2 * j + 2, len(fpieces))):
                        gp.dma_start(out=fpieces[pc][0],
                                     in_=fpieces[pc][1]).then_inc(sem["g_fc"], 16)
                    gp.wait_ge(sem["v_xn"], j + 1)
                    gp.dma_start(
                        out=hbm_chunk(xn_scr, j),
                        in_=xn_buf[j % 2].rearrange("p (g t) -> p g t", t=TC),
                    ).then_inc(sem[f"g_xn{j % 2}"], 16)
                    if j + 2 < NCH:
                        gp.wait_ge(*sa_x(j + 2))
                        gp.wait_ge(*g_xn_done(j))
                        gp.wait_ge(sem["t_rog"], 8 * j + 8)
                        gp.tensor_mul(xn_buf[(j + 2) % 2], x_buf[(j + 2) % 3],
                                      x_buf[(j + 2) % 3]).then_inc(sem["v_sq"], 1)
                    gp.wait_ge(sem["v_res"], 8 * j + 8)
                    gp.dma_start(
                        out=hbm_chunk(pt_scr, j),
                        in_=x_buf[j % 3].rearrange("p (g t) -> p g t", t=TC),
                    ).then_inc(sem[f"g_part{j % 2}"], 16)
                gp.wait_ge(sem["g_xn0"], 16 * (NCH // 2))
                gp.wait_ge(sem["g_xn1"], 16 * (NCH // 2))
                gp.wait_ge(sem["g_part0"], 16 * (NCH // 2))
                gp.wait_ge(sem["g_part1"], 16 * (NCH // 2))
                gp.wait_ge(sem["g_fc"], 16 * 12)
                gp.sem_inc(sem["ph"], 1)

        sA.close()

        sB = ExitStack()
        wgbf = sB.enter_context(nc.sbuf_tensor("wgbf", [D, G * HID], BF16)).ap()
        wubf = sB.enter_context(nc.sbuf_tensor("wubf", [D, G * HID], BF16)).ap()
        wdbf = sB.enter_context(nc.sbuf_tensor("wdbf", [D, HG * 1024], BF16)).ap()
        xnb = sB.enter_context(nc.sbuf_tensor("xnb", [D, G * TC], BF16)).ap()
        pb = sB.enter_context(nc.sbuf_tensor("pb", [D, G * TC], F32)).ap()
        h_act = sB.enter_context(nc.sbuf_tensor("h_act", [D, HG * TC], BF16)).ap()
        silu = [sB.enter_context(nc.sbuf_tensor(f"silu{i}", [D, TC], F32)).ap()
                for i in range(2)]
        psB_gu = sB.enter_context(nc.psum_tensor("psB_gu", [D, 4 * TC], F32)).ap()
        psB_d = sB.enter_context(nc.psum_tensor("psB_d", [D, 4 * TC], F32)).ap()

        def gate_sl(hg):
            return psB_gu[:, (hg % 2) * 2 * TC:(hg % 2) * 2 * TC + TC]

        def up_sl(hg):
            return psB_gu[:, (hg % 2) * 2 * TC + TC:(hg % 2) * 2 * TC + 2 * TC]

        with nc.Block() as blk2:

            @blk2.sync
            def _(sy):
                sy.wait_ge(sem["ph"], 1)
                sy.dma_start(
                    out=wgbf.rearrange("p (g n) -> p g n", n=HID)[:, :, 0:1536],
                    in_=wg_bfd.rearrange("(g p) n -> p g n", p=D)[:, :, 0:1536],
                ).then_inc(sem["g_wgb"], 16)
                sy.dma_start(
                    out=wubf.rearrange("p (g n) -> p g n", n=HID)[:, :, 0:1536],
                    in_=wu_bfd.rearrange("(g p) n -> p g n", p=D)[:, :, 0:1536],
                ).then_inc(sem["g_wub"], 16)
                sy.dma_start(
                    out=wdbf.rearrange("p (g n) -> p g n", n=1024)[:, 0:12, :],
                    in_=wd_bfd.rearrange("(g p) n -> p g n", p=D)[:, 0:12, :],
                ).then_inc(sem["g_wdb"], 16)
                sy.dma_start(
                    out=wgbf.rearrange("p (g n) -> p g n", n=HID)[:, :, 1536:HID],
                    in_=wg_bfd.rearrange("(g p) n -> p g n", p=D)[:, :, 1536:HID],
                ).then_inc(sem["g_wgb2"], 16)
                sy.dma_start(
                    out=wubf.rearrange("p (g n) -> p g n", n=HID)[:, :, 1536:HID],
                    in_=wu_bfd.rearrange("(g p) n -> p g n", p=D)[:, :, 1536:HID],
                ).then_inc(sem["g_wub2"], 16)
                sy.dma_start(
                    out=wdbf.rearrange("p (g n) -> p g n", n=1024)[:, 12:HG, :],
                    in_=wd_bfd.rearrange("(g p) n -> p g n", p=D)[:, 12:HG, :],
                ).then_inc(sem["g_wdb2"], 16)
                for j in range(NCH):
                    if j >= 1:
                        sy.wait_ge(sem["t_u"], 24 * j)
                    sy.dma_start(
                        out=xnb.rearrange("p (g t) -> p g t", t=TC),
                        in_=hbm_chunk(xn_scr, j)).then_inc(sem["sB_xn"], 16)
                    if j >= 1:
                        sy.wait_ge(sem["gB_out"], 16 * j)
                    sy.dma_start(
                        out=pb.rearrange("p (g t) -> p g t", t=TC),
                        in_=hbm_chunk(pt_scr, j)).then_inc(sem["sB_pb"], 16)

            @blk2.vector
            def _(ve):
                ve.wait_ge(sem["ph"], 1)
                for j in range(NCH):
                    for hg in range(HG):
                        idx = 24 * j + hg
                        ve.wait_ge(sem["s_sil"], idx + 1)
                        ve.wait_ge(sem["t_u"], idx + 1)
                        ve.scalar_tensor_tensor(
                            out=h_act[:, hg * TC:(hg + 1) * TC], in0=up_sl(hg),
                            scalar=1.0 / 127.0, in1=silu[hg % 2], op0=Alu.mult,
                            op1=Alu.mult).then_inc(sem["v_hact"], 1)
                    ve.wait_ge(sem["t_p1"], j + 1)
                    ve.wait_ge(sem["sB_pb"], 16 * (j + 1))
                    for dg in range(4):
                        ins = ve.scalar_tensor_tensor(
                            out=gsl(pb, dg), in0=psB_d[:, dg * TC:(dg + 1) * TC],
                            scalar=1.0 / 127.0, in1=gsl(pb, dg),
                            op0=Alu.mult, op1=Alu.add)
                    ins.then_inc(sem["v_res1"], 1)
                    ve.wait_ge(sem["t_p2"], j + 1)
                    for dg in range(4, 8):
                        ins = ve.scalar_tensor_tensor(
                            out=gsl(pb, dg), in0=psB_d[:, (dg - 4) * TC:(dg - 3) * TC],
                            scalar=1.0 / 127.0, in1=gsl(pb, dg),
                            op0=Alu.mult, op1=Alu.add)
                    ins.then_inc(sem["v_res2"], 1)

            @blk2.scalar
            def _(sc):
                sc.wait_ge(sem["ph"], 1)
                for j in range(NCH):
                    for hg in range(HG):
                        idx = 24 * j + hg
                        if idx >= 2:
                            sc.wait_ge(sem["v_hact"], idx - 1)
                        sc.wait_ge(sem["t_g"], idx + 1)
                        sc.activation(out=silu[hg % 2], in_=gate_sl(hg), func=Act.Silu,
                                      scale=1.0 / 127.0).then_inc(sem["s_sil"], 1)

            @blk2.tensor
            def _(te):
                te.wait_ge(sem["ph"], 1)
                for j in range(NCH):
                    te.wait_ge(sem["sB_xn"], 16 * (j + 1))
                    for hg in range(HG):
                        idx = 24 * j + hg
                        if j == 0:
                            if hg < 12:
                                te.wait_ge(sem["g_wgb"], 16)
                            else:
                                te.wait_ge(sem["g_wgb2"], 16)
                        if idx >= 2:
                            te.wait_ge(sem["v_hact"], idx - 1)
                        for gc in range(G):
                            ins = te.matmul(
                                gate_sl(hg),
                                wgbf[:, gc * HID + hg * D:gc * HID + (hg + 1) * D],
                                gsl(xnb, gc), start=(gc == 0), stop=(gc == G - 1))
                        ins.then_inc(sem["t_g"], 1)
                        if j == 0:
                            if hg < 12:
                                te.wait_ge(sem["g_wub"], 16)
                            else:
                                te.wait_ge(sem["g_wub2"], 16)
                        for gc in range(G):
                            ins = te.matmul(
                                up_sl(hg),
                                wubf[:, gc * HID + hg * D:gc * HID + (hg + 1) * D],
                                gsl(xnb, gc), start=(gc == 0), stop=(gc == G - 1))
                        ins.then_inc(sem["t_u"], 1)
                        if hg >= 1:
                            ph_hg = hg - 1
                            if j == 0:
                                if ph_hg < 12:
                                    te.wait_ge(sem["g_wdb"], 16)
                                else:
                                    te.wait_ge(sem["g_wdb2"], 16)
                            if ph_hg == 0 and j >= 1:
                                te.wait_ge(sem["v_res2"], j)
                            te.wait_ge(sem["v_hact"], 24 * j + ph_hg + 1)
                            for dg in range(4):
                                te.matmul(
                                    psB_d[:, dg * TC:(dg + 1) * TC],
                                    wdbf[:, ph_hg * 1024 + dg * D:ph_hg * 1024 + (dg + 1) * D],
                                    h_act[:, ph_hg * TC:(ph_hg + 1) * TC],
                                    start=(ph_hg == 0), stop=False)
                    te.wait_ge(sem["v_hact"], 24 * j + 24)
                    for dg in range(4):
                        ins = te.matmul(
                            psB_d[:, dg * TC:(dg + 1) * TC],
                            wdbf[:, 23 * 1024 + dg * D:23 * 1024 + (dg + 1) * D],
                            h_act[:, 23 * TC:24 * TC],
                            start=False, stop=True)
                    ins.then_inc(sem["t_p1"], 1)
                    if j == 0:
                        te.wait_ge(sem["g_wdb2"], 16)
                    te.wait_ge(sem["v_res1"], j + 1)
                    for hg in range(HG):
                        for dg in range(4, 8):
                            ins = te.matmul(
                                psB_d[:, (dg - 4) * TC:(dg - 3) * TC],
                                wdbf[:, hg * 1024 + dg * D:hg * 1024 + (dg + 1) * D],
                                h_act[:, hg * TC:(hg + 1) * TC],
                                start=(hg == 0), stop=(hg == HG - 1))
                    ins.then_inc(sem["t_p2"], 1)

            @blk2.gpsimd
            def _(gp):
                for j in range(NCH):
                    gp.wait_ge(sem["v_res2"], j + 1)
                    gp.dma_start(
                        out=hbm_chunk(out_hbm, j),
                        in_=pb.rearrange("p (g t) -> p g t", t=TC),
                    ).then_inc(sem["gB_out"], 16)
                gp.wait_ge(sem["gB_out"], 16 * NCH)

        sB.close()
    return nc


def kernel(L0_attn_x, L0_attn_Wk, L0_attn_Wr, L0_attn_Wo,
           L0_attn_time_decay, L0_attn_time_first,
           L0_ffn_w_gate, L0_ffn_w_up, L0_ffn_w_down):
    x = np.asarray(L0_attn_x, dtype=np.float32)
    wkT = np.ascontiguousarray(np.asarray(L0_attn_Wk, dtype=np.float32).T)
    wrT = np.ascontiguousarray(np.asarray(L0_attn_Wr, dtype=np.float32).T)
    woT = np.ascontiguousarray(np.asarray(L0_attn_Wo, dtype=np.float32).T)
    td = np.ascontiguousarray(np.asarray(L0_attn_time_decay, dtype=np.float32).reshape(G, D).T)
    tf = np.ascontiguousarray(np.asarray(L0_attn_time_first, dtype=np.float32).reshape(G, D).T)
    wgT = np.ascontiguousarray(np.asarray(L0_ffn_w_gate, dtype=np.int32).T)
    wuT = np.ascontiguousarray(np.asarray(L0_ffn_w_up, dtype=np.int32).T)
    wdT = np.ascontiguousarray(np.asarray(L0_ffn_w_down, dtype=np.int32).T)

    nc = build_nc()
    in_maps = []
    for b in range(NCORE):
        in_maps.append({
            "x_cm": np.ascontiguousarray(x[b].T),
            "wkT": wkT, "wrT": wrT, "woT": woT,
            "td_in": td, "tf_in": tf,
            "wgT": wgT, "wuT": wuT, "wdT": wdT,
        })
    res = run_bass_kernel_spmd(nc, in_maps, core_ids=list(range(NCORE)))
    out = np.stack([np.ascontiguousarray(r["out_cm"].T) for r in res.results])
    return out


# version 11
# speedup vs baseline: 1.0225x; 1.0225x over previous
import numpy as np
import ml_dtypes
import concourse.bass as bass
import concourse.mybir as mybir
from concourse.bass_utils import run_bass_kernel_spmd
from contextlib import ExitStack

F32 = mybir.dt.float32
BF16 = mybir.dt.bfloat16
F32R = mybir.dt.float32r
I32 = mybir.dt.int32
Act = mybir.ActivationFunctionType
Alu = mybir.AluOpType
AX = mybir.AxisListType.X

NCORE = 8
T = 4096
C = 1024
HID = 3072
G = 8
HG = 24
TC = 512
NCH = T // TC
D = 128
P_INIT = -1.0e30


def build_nc():
    nc = bass.Bass("TRN2", target_bir_lowering=False)

    x_hbm = nc.dram_tensor("x_cm", [C, T], F32, kind="ExternalInput").ap()
    wk_hbm = nc.dram_tensor("wkT", [C, C], F32, kind="ExternalInput").ap()
    wr_hbm = nc.dram_tensor("wrT", [C, C], F32, kind="ExternalInput").ap()
    wo_hbm = nc.dram_tensor("woT", [C, C], F32, kind="ExternalInput").ap()
    td_hbm = nc.dram_tensor("td_in", [D, G], F32, kind="ExternalInput").ap()
    tf_hbm = nc.dram_tensor("tf_in", [D, G], F32, kind="ExternalInput").ap()
    wg_hbm = nc.dram_tensor("wgT", [C, HID], I32, kind="ExternalInput").ap()
    wu_hbm = nc.dram_tensor("wuT", [C, HID], I32, kind="ExternalInput").ap()
    wd_hbm = nc.dram_tensor("wdT", [HID, C], I32, kind="ExternalInput").ap()
    out_hbm = nc.dram_tensor("out_cm", [C, T], F32, kind="ExternalOutput").ap()
    xn_scr = nc.dram_tensor("xn_scr", [C, T], BF16, kind="Internal").ap()
    wg_bfd = nc.dram_tensor("wg_bfd", [C, HID], BF16, kind="Internal").ap()
    wu_bfd = nc.dram_tensor("wu_bfd", [C, HID], BF16, kind="Internal").ap()
    wd_bfd = nc.dram_tensor("wd_bfd", [HID, C], BF16, kind="Internal").ap()
    pt_scr = nc.dram_tensor("pt_scr", [C, T], F32, kind="Internal").ap()

    def hbm_chunk(ap_, j, dt_cols=TC):
        return ap_[:, j * dt_cols:(j + 1) * dt_cols].rearrange("(g p) t -> p g t", p=D)

    es = ExitStack()
    with es:
        sem = {}
        for name in [
            "sPio", "g_wk", "g_wr", "g_wo", "g_fc", "s_pexp", "v_pw", "s_pew",
            "sA_x0", "sA_x1", "sA_x2", "v_sq", "t_stats", "v_var", "s_srow", "v_sv",
            "t_bcs", "s_sev", "t_bcm", "s_msev", "v_xn", "g_xn0", "g_xn1",
            "t_kog", "t_rog", "s_kev", "s_rev", "v_m", "s_es",
            "s_act", "v_den", "s_rec", "v_out", "t_wo", "v_res", "g_part0", "g_part1",
            "ph",
            "g_wgb", "g_wub", "g_wdb", "g_wgb2", "g_wub2", "g_wdb2", "sB_xn", "sB_pb", "t_g", "t_u", "s_sil",
            "v_hact", "t_p1", "v_res1", "t_p2", "v_res2", "gB_out",
        ]:
            sem[name] = es.enter_context(nc.semaphore(name))

        def sa_x(j):
            return sem[f"sA_x{j % 3}"], 16 * (j // 3 + 1)

        def g_xn_done(j):
            return sem[f"g_xn{j % 2}"], 16 * (j // 2 + 1)

        def g_part_done(j):
            return sem[f"g_part{j % 2}"], 16 * (j // 2 + 1)

        sA = ExitStack()
        tdt = sA.enter_context(nc.sbuf_tensor("tdt", [D, G], F32)).ap()
        tft = sA.enter_context(nc.sbuf_tensor("tft", [D, G], F32)).ap()
        exptd = sA.enter_context(nc.sbuf_tensor("exptd", [D, G], F32)).ap()
        wv = sA.enter_context(nc.sbuf_tensor("wv", [D, G], F32)).ap()
        ewv = sA.enter_context(nc.sbuf_tensor("ewv", [D, G], F32)).ap()
        euv = sA.enter_context(nc.sbuf_tensor("euv", [D, G], F32)).ap()
        relu_u = sA.enter_context(nc.sbuf_tensor("relu_u", [D, G], F32)).ap()
        a_all = sA.enter_context(nc.sbuf_tensor("a_all", [D, G], F32)).ap()
        b_all = sA.enter_context(nc.sbuf_tensor("b_all", [D, G], F32)).ap()
        Mall = sA.enter_context(nc.sbuf_tensor("Mall", [D, G], F32)).ap()
        t1b = sA.enter_context(nc.sbuf_tensor("t1b", [D, G], F32)).ap()
        m_all2 = [sA.enter_context(nc.sbuf_tensor(f"m_all{i}", [D, G], F32)).ap() for i in range(2)]
        neg_m2 = [sA.enter_context(nc.sbuf_tensor(f"neg_m{i}", [D, G], F32)).ap() for i in range(2)]
        dp2 = [sA.enter_context(nc.sbuf_tensor(f"dp{i}", [D, G], F32)).ap() for i in range(2)]
        esb2 = [sA.enter_context(nc.sbuf_tensor(f"esb{i}", [D, G], F32)).ap() for i in range(2)]
        a02 = [sA.enter_context(nc.sbuf_tensor(f"a0{i}", [D, G], F32)).ap() for i in range(2)]
        b02 = [sA.enter_context(nc.sbuf_tensor(f"b0{i}", [D, G], F32)).ap() for i in range(2)]
        onescol_bf = sA.enter_context(nc.sbuf_tensor("onescol_bf", [D, 1], BF16)).ap()
        onescol_f = sA.enter_context(nc.sbuf_tensor("onescol_f", [D, 1], F32)).ap()
        eps_t = sA.enter_context(nc.sbuf_tensor("eps_t", [D, 1], F32)).ap()
        nc.const_aps.register(F32, 1e-5, eps_t)
        ones1 = sA.enter_context(nc.sbuf_tensor("ones1", [1, D], F32)).ap()
        ew_full = sA.enter_context(nc.sbuf_tensor("ew_full", [D, G * TC], F32)).ap()
        wk_bf = sA.enter_context(nc.sbuf_tensor("wk_bf", [D, G * 1024], BF16)).ap()
        wr_bf = sA.enter_context(nc.sbuf_tensor("wr_bf", [D, G * 1024], BF16)).ap()
        wo_bf = sA.enter_context(nc.sbuf_tensor("wo_bf", [D, G * 1024], BF16)).ap()
        x_buf = [sA.enter_context(nc.sbuf_tensor(f"x_buf{i}", [D, G * TC], F32)).ap()
                 for i in range(3)]
        xn_buf = [sA.enter_context(nc.sbuf_tensor(f"xn_buf{i}", [D, G * TC], BF16)).ap()
                  for i in range(2)]
        k_sb = sA.enter_context(nc.sbuf_tensor("k_sb", [D, G * TC], F32)).ap()
        r_sb = sA.enter_context(nc.sbuf_tensor("r_sb", [D, G * TC], F32)).ap()
        outpre = sA.enter_context(nc.sbuf_tensor("outpre", [D, G * TC], BF16)).ap()
        rows = sA.enter_context(nc.sbuf_tensor("rows", [1, 3 * TC], F32)).ap()
        m_row = rows[0:1, 0:TC]
        msq_row = rows[0:1, TC:2 * TC]
        s_row = rows[0:1, 2 * TC:3 * TC]
        ms_row = m_row
        var_row = s_row
        sb_sb = sA.enter_context(nc.sbuf_tensor("sb_sb", [D, TC], F32)).ap()
        msb_sb = sA.enter_context(nc.sbuf_tensor("msb_sb", [D, TC], F32)).ap()
        ek_s = [sA.enter_context(nc.sbuf_tensor(f"ek{i}", [D, TC], F32)).ap() for i in range(3)]
        eu_1 = sA.enter_context(nc.sbuf_tensor("eu_1", [D, TC], F32)).ap()
        enr_s = [sA.enter_context(nc.sbuf_tensor(f"enr{i}", [D, TC], F32)).ap() for i in range(3)]
        zA_s = [sA.enter_context(nc.sbuf_tensor(f"zA{i}", [D, TC], F32)).ap() for i in range(2)]
        den_1 = sA.enter_context(nc.sbuf_tensor("den_1", [D, TC], F32)).ap()
        SA_e = [sA.enter_context(nc.sbuf_tensor(f"SA{i}", [D, TC + 4], F32)).ap() for i in range(2)]
        SB_e = [sA.enter_context(nc.sbuf_tensor(f"SB{i}", [D, TC + 4], F32)).ap() for i in range(2)]
        psA_st = sA.enter_context(nc.psum_tensor("psA_st", [64, TC], F32)).ap()
        psA_bc = sA.enter_context(nc.psum_tensor("psA_bc", [D, TC], F32)).ap()
        ps_k = sA.enter_context(nc.psum_tensor("ps_k", [D, 2 * TC], F32)).ap()
        ps_r = sA.enter_context(nc.psum_tensor("ps_r", [D, 2 * TC], F32)).ap()
        ps_wo = sA.enter_context(nc.psum_tensor("ps_wo", [D, 2 * TC], F32)).ap()

        def gsl(ap_, g, w=TC):
            return ap_[:, g * w:(g + 1) * w]

        with nc.Block() as blk:

            @blk.sync
            def _(sy):
                sy.dma_start(out=tdt, in_=td_hbm).then_inc(sem["sPio"], 16)
                sy.dma_start(out=tft, in_=tf_hbm).then_inc(sem["sPio"], 16)
                for jj in range(3):
                    sy.dma_start(
                        out=x_buf[jj].rearrange("p (g t) -> p g t", t=TC),
                        in_=hbm_chunk(x_hbm, jj)).then_inc(sem[f"sA_x{jj % 3}"], 16)
                for jj in range(3, NCH):
                    sy.wait_ge(*g_part_done(jj - 3))
                    sy.dma_start(
                        out=x_buf[jj % 3].rearrange("p (g t) -> p g t", t=TC),
                        in_=hbm_chunk(x_hbm, jj)).then_inc(sem[f"sA_x{jj % 3}"], 16)

            @blk.scalar
            def _(sc):
                sc.wait_ge(sem["sPio"], 32)
                sc.activation(out=exptd, in_=tdt, func=Act.Exp).then_inc(sem["s_pexp"], 1)
                sc.wait_ge(sem["v_pw"], 1)
                sc.activation(out=ewv, in_=wv, func=Act.Exp)
                sc.activation(out=euv, in_=tft,
                              func=Act.Exp).then_inc(sem["s_pew"], 1)
                sc.wait_ge(sem["v_m"], 1)
                sc.activation(out=esb2[0], in_=dp2[0], func=Act.Exp).then_inc(sem["s_es"], 1)
                sc.wait_ge(sem["v_var"], 1)
                sc.activation(out=msq_row, in_=var_row, func=Act.Ln, bias=1e-5)
                sc.drain()
                sc.activation(out=s_row, in_=msq_row, func=Act.Exp,
                              scale=-0.5).then_inc(sem["s_srow"], 1)
                sc.wait_ge(sem["t_bcs"], 1)
                sc.copy(out=sb_sb, in_=psA_bc).then_inc(sem["s_sev"], 1)
                sc.wait_ge(sem["t_bcm"], 1)
                sc.copy(out=msb_sb, in_=psA_bc).then_inc(sem["s_msev"], 1)

                for j in range(NCH):
                    q = j % 2
                    for og in range(G):
                        idx = 8 * j + og
                        if j >= 1:
                            sc.wait_ge(sem["v_den"], 8 * (j - 1) + og + 1)
                        sc.wait_ge(sem["t_kog"], idx + 1)
                        sc.copy(out=gsl(k_sb, og), in_=gsl(ps_k, og % 2)).then_inc(sem["s_kev"], 1)
                        sc.wait_ge(sem["t_rog"], idx + 1)
                        sc.copy(out=gsl(r_sb, og), in_=gsl(ps_r, og % 2)).then_inc(sem["s_rev"], 1)
                        if idx >= 3:
                            sc.wait_ge(sem["v_out"], idx - 2)
                            sc.wait_ge(sem["v_den"], idx - 2)
                        sc.drain()
                        sc.activation(out=ek_s[idx % 3], in_=gsl(k_sb, og), func=Act.Exp,
                                      bias=neg_m2[q][:, og:og + 1])
                        sc.activation(out=enr_s[idx % 3], in_=gsl(r_sb, og), func=Act.Exp,
                                      scale=-1.0).then_inc(sem["s_act"], 1)
                        if og >= 1:
                            pidx = idx - 1
                            sc.wait_ge(sem["v_den"], pidx + 1)
                            sc.activation(out=eu_1, in_=den_1, func=Act.Ln)
                            sc.drain()
                            sc.activation(out=ek_s[(idx - 1) % 3], in_=eu_1, func=Act.Exp,
                                          scale=-1.0).then_inc(sem["s_rec"], 1)
                    if j + 1 < NCH:
                        sc.wait_ge(sem["v_var"], j + 2)
                        sc.activation(out=msq_row, in_=var_row, func=Act.Ln, bias=1e-5)
                        sc.drain()
                        sc.activation(out=s_row, in_=msq_row, func=Act.Exp,
                                      scale=-0.5).then_inc(sem["s_srow"], 1)
                    pidx = 8 * j + 7
                    sc.wait_ge(sem["v_den"], pidx + 1)
                    sc.activation(out=eu_1, in_=den_1, func=Act.Ln)
                    sc.drain()
                    sc.activation(out=ek_s[pidx % 3], in_=eu_1, func=Act.Exp,
                                  scale=-1.0).then_inc(sem["s_rec"], 1)
                    if j + 1 < NCH:
                        q1 = (j + 1) % 2
                        sc.wait_ge(sem["v_m"], j + 2)
                        sc.activation(out=esb2[q1], in_=dp2[q1],
                                      func=Act.Exp).then_inc(sem["s_es"], 1)
                        sc.wait_ge(sem["t_bcs"], j + 2)
                        sc.copy(out=sb_sb, in_=psA_bc).then_inc(sem["s_sev"], 1)
                        sc.wait_ge(sem["t_bcm"], j + 2)
                        sc.copy(out=msb_sb, in_=psA_bc).then_inc(sem["s_msev"], 1)

            @blk.vector
            def _(ve):
                ve.memset(onescol_bf, 1.0)
                ve.memset(onescol_f, 1.0)
                ve.memset(eps_t, 1e-5)
                ve.memset(ones1, 1.0)
                ve.memset(a_all, 0.0)
                ve.memset(b_all, 0.0)
                ve.memset(Mall, 0.0)
                ve.memset(m_all2[1], P_INIT)
                ve.memset(den_1, 1.0)
                ve.drain()
                ve.wait_ge(sem["s_pexp"], 1)
                ve.tensor_scalar_mul(wv, exptd, -1.0).then_inc(sem["v_pw"], 1)
                ve.drain()
                ve.wait_ge(sem["s_pew"], 1)
                for g in range(G):
                    ve.tensor_scalar_mul(gsl(ew_full, g), den_1, ewv[:, g:g + 1])
                ve.wait_ge(sem["sPio"], 32)
                ve.tensor_scalar_max(relu_u, tft, 0.0)
                ve.drain()
                ve.tensor_add(t1b, Mall, relu_u)
                ve.drain()
                ve.tensor_max(m_all2[0], t1b, m_all2[1])
                ve.drain()
                ve.tensor_scalar_mul(neg_m2[0], m_all2[0], -1.0)
                ve.tensor_sub(dp2[0], m_all2[1], m_all2[0]).then_inc(sem["v_m"], 1)
                ve.wait_ge(*sa_x(0))
                ve.tensor_mul(xn_buf[0], x_buf[0], x_buf[0]).then_inc(sem["v_sq"], 1)
                ve.wait_ge(*sa_x(1))
                ve.tensor_mul(xn_buf[1], x_buf[1], x_buf[1]).then_inc(sem["v_sq"], 1)
                ve.wait_ge(sem["s_es"], 1)
                ve.tensor_mul(a02[0], a_all, esb2[0])
                ve.tensor_mul(b02[0], b_all, esb2[0])
                ve.wait_ge(sem["t_stats"], 1)
                ve.tensor_scalar_mul(m_row, psA_st[0:1, :], 1.0 / C)
                ve.tensor_scalar_mul(var_row, psA_st[32:33, :], 1.0 / C)
                ve.drain()
                ve.tensor_mul(msq_row, m_row, m_row)
                ve.drain()
                ve.scalar_tensor_tensor(
                    out=var_row, in0=msq_row, scalar=-1.0, in1=var_row,
                    op0=Alu.mult, op1=Alu.add).then_inc(sem["v_var"], 1)
                ve.wait_ge(sem["s_srow"], 1)
                ve.tensor_mul(ms_row, m_row, s_row).then_inc(sem["v_sv"], 1)
                ve.wait_ge(sem["s_msev"], 1)
                xtmp = [zA_s[0], zA_s[1], den_1, eu_1,
                        SA_e[0][:, 0:TC], SA_e[1][:, 0:TC],
                        SB_e[0][:, 0:TC], SB_e[1][:, 0:TC]]
                for g in range(G):
                    ve.tensor_mul(xtmp[g], gsl(x_buf[0], g), sb_sb)
                ve.drain()
                for g in range(G - 1):
                    ve.tensor_sub(gsl(xn_buf[0], g), xtmp[g], msb_sb)
                ve.tensor_sub(gsl(xn_buf[0], G - 1), xtmp[G - 1],
                              msb_sb).then_inc(sem["v_xn"], 1)

                for j in range(NCH):
                    q = j % 2
                    q1 = (j + 1) % 2
                    pj3 = j % 3

                    def outpre_g(g):
                        idx = 8 * j + g
                        pg = g % 2
                        if g == 0:
                            ve.wait_ge(sem["t_wo"], 8 * j)
                        ve.wait_ge(sem["s_rec"], idx + 1)
                        ve.tensor_mul(gsl(outpre, g), zA_s[pg],
                                      ek_s[idx % 3]).then_inc(sem["v_out"], 1)

                    def scan_group(g):
                        idx = 8 * j + g
                        pg = g % 2
                        ve.wait_ge(sem["s_act"], idx + 1)
                        if g >= 2:
                            outpre_g(g - 2)
                        ve.tensor_copy(SA_e[pg][:, 0:1], a02[q][:, g:g + 1])
                        ve.tensor_copy(SB_e[pg][:, 0:1], b02[q][:, g:g + 1])
                        ve.drain()
                        ve.tensor_mul(zA_s[pg], ek_s[idx % 3], gsl(r_sb, g))
                        ve.drain()
                        ve.tensor_tensor_scan(
                            out=SA_e[pg][:, 1:TC + 1], data0=gsl(ew_full, g),
                            data1=zA_s[pg], initial=a02[q][:, g:g + 1],
                            op0=Alu.mult, op1=Alu.add)
                        ve.tensor_tensor_scan(
                            out=SB_e[pg][:, 1:TC + 1], data0=gsl(ew_full, g),
                            data1=ek_s[idx % 3], initial=b02[q][:, g:g + 1],
                            op0=Alu.mult, op1=Alu.add)
                        ve.drain()
                        ve.scalar_tensor_tensor(
                            out=zA_s[pg], in0=zA_s[pg], scalar=euv[:, g:g + 1],
                            in1=SA_e[pg][:, 0:TC], op0=Alu.mult, op1=Alu.add)
                        if idx >= 1:
                            ve.wait_ge(sem["s_rec"], idx)
                        ve.scalar_tensor_tensor(
                            out=den_1, in0=ek_s[idx % 3], scalar=euv[:, g:g + 1],
                            in1=SB_e[pg][:, 0:TC], op0=Alu.mult, op1=Alu.add)
                        ve.tensor_copy(a_all[:, g:g + 1], SA_e[pg][:, TC:TC + 1])
                        ve.tensor_copy(b_all[:, g:g + 1], SB_e[pg][:, TC:TC + 1])
                        ve.drain()
                        ve.scalar_tensor_tensor(
                            out=den_1, in0=enr_s[idx % 3], scalar=1.0,
                            in1=den_1, op0=Alu.add,
                            op1=Alu.mult).then_inc(sem["v_den"], 1)

                    def resid_pair(i):
                        for cg in (2 * i, 2 * i + 1):
                            ve.wait_ge(sem["t_wo"], 8 * (j - 1) + cg + 1)
                            ve.tensor_add(gsl(x_buf[(j - 1) % 3], cg),
                                          gsl(x_buf[(j - 1) % 3], cg),
                                          gsl(ps_wo, cg % 2)).then_inc(sem["v_res"], 1)
                        ve.drain()

                    for g in range(6):
                        if j >= 1 and g < 4:
                            resid_pair(g)
                        scan_group(g)
                    if j + 1 < NCH:
                        ve.wait_ge(sem["t_stats"], j + 2)
                        ve.tensor_scalar_mul(m_row, psA_st[0:1, :], 1.0 / C)
                        ve.tensor_scalar_mul(var_row, psA_st[32:33, :], 1.0 / C)
                        ve.drain()
                        ve.tensor_mul(msq_row, m_row, m_row)
                        ve.drain()
                        ve.scalar_tensor_tensor(
                            out=var_row, in0=msq_row, scalar=-1.0, in1=var_row,
                            op0=Alu.mult, op1=Alu.add).then_inc(sem["v_var"], 1)
                    scan_group(6)
                    if j + 1 < NCH:
                        ve.wait_ge(sem["s_srow"], j + 2)
                        ve.tensor_mul(ms_row, m_row, s_row).then_inc(sem["v_sv"], 1)
                    scan_group(7)
                    outpre_g(G - 2)
                    outpre_g(G - 1)
                    ve.drain()
                    for og in range(G):
                        ve.wait_ge(sem["s_kev"], 8 * j + og + 1)
                        ve.reduce_max(Mall[:, og:og + 1], gsl(k_sb, og), axis=AX)
                    if j + 1 < NCH:
                        ve.drain()
                        ve.tensor_add(t1b, Mall, relu_u)
                        ve.drain()
                        ve.tensor_max(m_all2[q1], t1b, m_all2[q])
                        ve.drain()
                        ve.tensor_scalar_mul(neg_m2[q1], m_all2[q1], -1.0)
                        ve.tensor_sub(dp2[q1], m_all2[q], m_all2[q1]).then_inc(sem["v_m"], 1)
                        ve.wait_ge(sem["s_es"], j + 2)
                        ve.drain()
                        ve.tensor_mul(a02[q1], a_all, esb2[q1])
                        ve.tensor_mul(b02[q1], b_all, esb2[q1])
                        ve.wait_ge(sem["s_msev"], j + 2)
                        if j >= 1:
                            ve.wait_ge(*g_xn_done(j - 1))
                        ve.drain()
                        for g in range(G):
                            ve.tensor_mul(xtmp[g], gsl(x_buf[(j + 1) % 3], g), sb_sb)
                        ve.drain()
                        for g in range(G - 1):
                            ve.tensor_sub(gsl(xn_buf[q1], g), xtmp[g], msb_sb)
                        ve.tensor_sub(gsl(xn_buf[q1], G - 1), xtmp[G - 1],
                                      msb_sb).then_inc(sem["v_xn"], 1)
                    if j + 2 < NCH:
                        ve.wait_ge(*sa_x(j + 2))
                        ve.wait_ge(*g_xn_done(j))
                        ve.tensor_mul(xn_buf[(j + 2) % 2], x_buf[(j + 2) % 3],
                                      x_buf[(j + 2) % 3]).then_inc(sem["v_sq"], 1)

                for cg in range(G):
                    ve.wait_ge(sem["t_wo"], 8 * 7 + cg + 1)
                    ve.tensor_add(gsl(x_buf[7 % 3], cg), gsl(x_buf[7 % 3], cg),
                                  gsl(ps_wo, cg % 2)).then_inc(sem["v_res"], 1)

            @blk.tensor
            def _(te):
                def stats(s):
                    te.wait_ge(*sa_x(s))
                    te.wait_ge(sem["v_sq"], s + 1)
                    if s >= 1:
                        te.wait_ge(sem["v_var"], s)
                    for g in range(G):
                        te.matmul(psA_st[0:1, :], onescol_f, gsl(x_buf[s % 3], g),
                                  start=(g == 0), stop=(g == G - 1))
                    for g in range(G):
                        ins = te.matmul(psA_st[32:33, :], onescol_bf, gsl(xn_buf[s % 2], g),
                                        start=(g == 0), stop=(g == G - 1))
                    ins.then_inc(sem["t_stats"], 1)

                def bcasts(s):
                    te.wait_ge(sem["v_sv"], s + 1)
                    if s >= 1:
                        te.wait_ge(sem["s_msev"], s)
                    te.matmul(psA_bc, ones1[0:1, :], s_row,
                              start=True, stop=True).then_inc(sem["t_bcs"], 1)
                    te.wait_ge(sem["s_sev"], s + 1)
                    te.matmul(psA_bc, ones1[0:1, :], ms_row,
                              start=True, stop=True).then_inc(sem["t_bcm"], 1)

                for j in range(NCH):
                    if j == 0:
                        stats(0)
                        bcasts(0)
                    def wo_cg(cg):
                        cgidx = 8 * (j - 1) + cg
                        if cgidx >= 2:
                            te.wait_ge(sem["v_res"], cgidx - 1)
                        for hg in range(G):
                            ins = te.matmul(
                                gsl(ps_wo, cg % 2),
                                wo_bf[:, hg * 1024 + cg * D:hg * 1024 + (cg + 1) * D],
                                gsl(outpre, hg),
                                start=(hg == 0), stop=(hg == G - 1))
                        ins.then_inc(sem["t_wo"], 1)

                    te.wait_ge(sem["v_xn"], j + 1)
                    if j == 0:
                        te.wait_ge(sem["g_wk"], 16)
                        te.wait_ge(sem["g_wr"], 16)
                    def proj_og(og):
                        ogidx = 8 * j + og
                        if ogidx >= 2:
                            te.wait_ge(sem["s_kev"], ogidx - 1)
                        for gk in range(G):
                            ins = te.matmul(
                                gsl(ps_k, og % 2),
                                wk_bf[:, gk * 1024 + og * D:gk * 1024 + (og + 1) * D],
                                gsl(xn_buf[j % 2], gk),
                                start=(gk == 0), stop=(gk == G - 1))
                        ins.then_inc(sem["t_kog"], 1)
                        if ogidx >= 2:
                            te.wait_ge(sem["s_rev"], ogidx - 1)
                        for gk in range(G):
                            ins = te.matmul(
                                gsl(ps_r, og % 2),
                                wr_bf[:, gk * 1024 + og * D:gk * 1024 + (og + 1) * D],
                                gsl(xn_buf[j % 2], gk),
                                start=(gk == 0), stop=(gk == G - 1))
                        ins.then_inc(sem["t_rog"], 1)

                    if j >= 1:
                        te.wait_ge(sem["v_out"], 8 * (j - 1) + 8)
                        if j == 1:
                            te.wait_ge(sem["g_wo"], 16)
                    for og in range(G):
                        if j >= 1 and og < 4:
                            wo_cg(2 * og)
                            wo_cg(2 * og + 1)
                        proj_og(og)
                    if j + 1 < NCH:
                        stats(j + 1)
                        bcasts(j + 1)
                    def wo_cg(cg):
                        cgidx = 8 * (j - 1) + cg
                te.wait_ge(sem["v_out"], 64)
                for cg in range(G):
                    cgidx = 8 * 7 + cg
                    if cgidx >= 2:
                        te.wait_ge(sem["v_res"], cgidx - 1)
                    for hg in range(G):
                        ins = te.matmul(
                            gsl(ps_wo, cg % 2),
                            wo_bf[:, hg * 1024 + cg * D:hg * 1024 + (cg + 1) * D],
                            gsl(outpre, hg),
                            start=(hg == 0), stop=(hg == G - 1))
                    ins.then_inc(sem["t_wo"], 1)

            @blk.gpsimd
            def _(gp):
                gp.wait_ge(sem["sA_x0"], 16)
                gp.dma_start(
                    out=wk_bf.rearrange("p (g n) -> p g n", n=1024),
                    in_=wk_hbm.rearrange("(g p) n -> p g n", p=D)).then_inc(sem["g_wk"], 16)
                gp.dma_start(
                    out=wr_bf.rearrange("p (g n) -> p g n", n=1024),
                    in_=wr_hbm.rearrange("(g p) n -> p g n", p=D)).then_inc(sem["g_wr"], 16)
                gp.dma_start(
                    out=wo_bf.rearrange("p (g n) -> p g n", n=1024),
                    in_=wo_hbm.rearrange("(g p) n -> p g n", p=D)).then_inc(sem["g_wo"], 16)
                fpieces = []
                for r in range(4):
                    fpieces.append((wg_bfd[r * 256:(r + 1) * 256, :],
                                    wg_hbm[r * 256:(r + 1) * 256, :]))
                for r in range(4):
                    fpieces.append((wu_bfd[r * 256:(r + 1) * 256, :],
                                    wu_hbm[r * 256:(r + 1) * 256, :]))
                for r in range(4):
                    fpieces.append((wd_bfd[r * 768:(r + 1) * 768, :],
                                    wd_hbm[r * 768:(r + 1) * 768, :]))
                for j in range(NCH):
                    for pc in range(2 * j, min(2 * j + 2, len(fpieces))):
                        gp.dma_start(out=fpieces[pc][0],
                                     in_=fpieces[pc][1]).then_inc(sem["g_fc"], 16)
                    gp.wait_ge(sem["v_xn"], j + 1)
                    gp.dma_start(
                        out=hbm_chunk(xn_scr, j),
                        in_=xn_buf[j % 2].rearrange("p (g t) -> p g t", t=TC),
                    ).then_inc(sem[f"g_xn{j % 2}"], 16)
                    gp.wait_ge(sem["v_res"], 8 * j + 8)
                    gp.dma_start(
                        out=hbm_chunk(pt_scr, j),
                        in_=x_buf[j % 3].rearrange("p (g t) -> p g t", t=TC),
                    ).then_inc(sem[f"g_part{j % 2}"], 16)
                gp.wait_ge(sem["g_xn0"], 16 * (NCH // 2))
                gp.wait_ge(sem["g_xn1"], 16 * (NCH // 2))
                gp.wait_ge(sem["g_part0"], 16 * (NCH // 2))
                gp.wait_ge(sem["g_part1"], 16 * (NCH // 2))
                gp.wait_ge(sem["g_fc"], 16 * 12)
                gp.sem_inc(sem["ph"], 1)

        sA.close()

        sB = ExitStack()
        wgbf = sB.enter_context(nc.sbuf_tensor("wgbf", [D, G * HID], BF16)).ap()
        wubf = sB.enter_context(nc.sbuf_tensor("wubf", [D, G * HID], BF16)).ap()
        wdbf = sB.enter_context(nc.sbuf_tensor("wdbf", [D, HG * 1024], BF16)).ap()
        xnb = sB.enter_context(nc.sbuf_tensor("xnb", [D, G * TC], BF16)).ap()
        pb = sB.enter_context(nc.sbuf_tensor("pb", [D, G * TC], F32)).ap()
        h_act = sB.enter_context(nc.sbuf_tensor("h_act", [D, HG * TC], BF16)).ap()
        silu = [sB.enter_context(nc.sbuf_tensor(f"silu{i}", [D, TC], F32)).ap()
                for i in range(2)]
        psB_gu = sB.enter_context(nc.psum_tensor("psB_gu", [D, 4 * TC], F32)).ap()
        psB_d = sB.enter_context(nc.psum_tensor("psB_d", [D, 4 * TC], F32)).ap()

        def gate_sl(hg):
            return psB_gu[:, (hg % 2) * 2 * TC:(hg % 2) * 2 * TC + TC]

        def up_sl(hg):
            return psB_gu[:, (hg % 2) * 2 * TC + TC:(hg % 2) * 2 * TC + 2 * TC]

        with nc.Block() as blk2:

            @blk2.sync
            def _(sy):
                sy.wait_ge(sem["ph"], 1)
                sy.dma_start(
                    out=wgbf.rearrange("p (g n) -> p g n", n=HID)[:, :, 0:1536],
                    in_=wg_bfd.rearrange("(g p) n -> p g n", p=D)[:, :, 0:1536],
                ).then_inc(sem["g_wgb"], 16)
                sy.dma_start(
                    out=wubf.rearrange("p (g n) -> p g n", n=HID)[:, :, 0:1536],
                    in_=wu_bfd.rearrange("(g p) n -> p g n", p=D)[:, :, 0:1536],
                ).then_inc(sem["g_wub"], 16)
                sy.dma_start(
                    out=wdbf.rearrange("p (g n) -> p g n", n=1024)[:, 0:12, :],
                    in_=wd_bfd.rearrange("(g p) n -> p g n", p=D)[:, 0:12, :],
                ).then_inc(sem["g_wdb"], 16)
                sy.dma_start(
                    out=wgbf.rearrange("p (g n) -> p g n", n=HID)[:, :, 1536:HID],
                    in_=wg_bfd.rearrange("(g p) n -> p g n", p=D)[:, :, 1536:HID],
                ).then_inc(sem["g_wgb2"], 16)
                sy.dma_start(
                    out=wubf.rearrange("p (g n) -> p g n", n=HID)[:, :, 1536:HID],
                    in_=wu_bfd.rearrange("(g p) n -> p g n", p=D)[:, :, 1536:HID],
                ).then_inc(sem["g_wub2"], 16)
                sy.dma_start(
                    out=wdbf.rearrange("p (g n) -> p g n", n=1024)[:, 12:HG, :],
                    in_=wd_bfd.rearrange("(g p) n -> p g n", p=D)[:, 12:HG, :],
                ).then_inc(sem["g_wdb2"], 16)
                for j in range(NCH):
                    if j >= 1:
                        sy.wait_ge(sem["t_u"], 24 * j)
                    sy.dma_start(
                        out=xnb.rearrange("p (g t) -> p g t", t=TC),
                        in_=hbm_chunk(xn_scr, j)).then_inc(sem["sB_xn"], 16)
                    if j >= 1:
                        sy.wait_ge(sem["gB_out"], 16 * j)
                    sy.dma_start(
                        out=pb.rearrange("p (g t) -> p g t", t=TC),
                        in_=hbm_chunk(pt_scr, j)).then_inc(sem["sB_pb"], 16)

            @blk2.vector
            def _(ve):
                ve.wait_ge(sem["ph"], 1)
                for j in range(NCH):
                    for hg in range(HG):
                        idx = 24 * j + hg
                        ve.wait_ge(sem["s_sil"], idx + 1)
                        ve.wait_ge(sem["t_u"], idx + 1)
                        ve.scalar_tensor_tensor(
                            out=h_act[:, hg * TC:(hg + 1) * TC], in0=up_sl(hg),
                            scalar=1.0 / 127.0, in1=silu[hg % 2], op0=Alu.mult,
                            op1=Alu.mult).then_inc(sem["v_hact"], 1)
                    ve.wait_ge(sem["t_p1"], j + 1)
                    ve.wait_ge(sem["sB_pb"], 16 * (j + 1))
                    for dg in range(4):
                        ins = ve.scalar_tensor_tensor(
                            out=gsl(pb, dg), in0=psB_d[:, dg * TC:(dg + 1) * TC],
                            scalar=1.0 / 127.0, in1=gsl(pb, dg),
                            op0=Alu.mult, op1=Alu.add)
                    ins.then_inc(sem["v_res1"], 1)
                    ve.wait_ge(sem["t_p2"], j + 1)
                    for dg in range(4, 8):
                        ins = ve.scalar_tensor_tensor(
                            out=gsl(pb, dg), in0=psB_d[:, (dg - 4) * TC:(dg - 3) * TC],
                            scalar=1.0 / 127.0, in1=gsl(pb, dg),
                            op0=Alu.mult, op1=Alu.add)
                    ins.then_inc(sem["v_res2"], 1)

            @blk2.scalar
            def _(sc):
                sc.wait_ge(sem["ph"], 1)
                for j in range(NCH):
                    for hg in range(HG):
                        idx = 24 * j + hg
                        if idx >= 2:
                            sc.wait_ge(sem["v_hact"], idx - 1)
                        sc.wait_ge(sem["t_g"], idx + 1)
                        sc.activation(out=silu[hg % 2], in_=gate_sl(hg), func=Act.Silu,
                                      scale=1.0 / 127.0).then_inc(sem["s_sil"], 1)

            @blk2.tensor
            def _(te):
                te.wait_ge(sem["ph"], 1)
                for j in range(NCH):
                    te.wait_ge(sem["sB_xn"], 16 * (j + 1))
                    for hg in range(HG):
                        idx = 24 * j + hg
                        if j == 0:
                            if hg < 12:
                                te.wait_ge(sem["g_wgb"], 16)
                            else:
                                te.wait_ge(sem["g_wgb2"], 16)
                        if idx >= 2:
                            te.wait_ge(sem["v_hact"], idx - 1)
                        for gc in range(G):
                            ins = te.matmul(
                                gate_sl(hg),
                                wgbf[:, gc * HID + hg * D:gc * HID + (hg + 1) * D],
                                gsl(xnb, gc), start=(gc == 0), stop=(gc == G - 1))
                        ins.then_inc(sem["t_g"], 1)
                        if j == 0:
                            if hg < 12:
                                te.wait_ge(sem["g_wub"], 16)
                            else:
                                te.wait_ge(sem["g_wub2"], 16)
                        for gc in range(G):
                            ins = te.matmul(
                                up_sl(hg),
                                wubf[:, gc * HID + hg * D:gc * HID + (hg + 1) * D],
                                gsl(xnb, gc), start=(gc == 0), stop=(gc == G - 1))
                        ins.then_inc(sem["t_u"], 1)
                        if hg >= 1:
                            ph_hg = hg - 1
                            if j == 0:
                                if ph_hg < 12:
                                    te.wait_ge(sem["g_wdb"], 16)
                                else:
                                    te.wait_ge(sem["g_wdb2"], 16)
                            if ph_hg == 0 and j >= 1:
                                te.wait_ge(sem["v_res2"], j)
                            te.wait_ge(sem["v_hact"], 24 * j + ph_hg + 1)
                            for dg in range(4):
                                te.matmul(
                                    psB_d[:, dg * TC:(dg + 1) * TC],
                                    wdbf[:, ph_hg * 1024 + dg * D:ph_hg * 1024 + (dg + 1) * D],
                                    h_act[:, ph_hg * TC:(ph_hg + 1) * TC],
                                    start=(ph_hg == 0), stop=False)
                    te.wait_ge(sem["v_hact"], 24 * j + 24)
                    for dg in range(4):
                        ins = te.matmul(
                            psB_d[:, dg * TC:(dg + 1) * TC],
                            wdbf[:, 23 * 1024 + dg * D:23 * 1024 + (dg + 1) * D],
                            h_act[:, 23 * TC:24 * TC],
                            start=False, stop=True)
                    ins.then_inc(sem["t_p1"], 1)
                    if j == 0:
                        te.wait_ge(sem["g_wdb2"], 16)
                    te.wait_ge(sem["v_res1"], j + 1)
                    for hg in range(HG):
                        for dg in range(4, 8):
                            ins = te.matmul(
                                psB_d[:, (dg - 4) * TC:(dg - 3) * TC],
                                wdbf[:, hg * 1024 + dg * D:hg * 1024 + (dg + 1) * D],
                                h_act[:, hg * TC:(hg + 1) * TC],
                                start=(hg == 0), stop=(hg == HG - 1))
                    ins.then_inc(sem["t_p2"], 1)

            @blk2.gpsimd
            def _(gp):
                for j in range(NCH):
                    gp.wait_ge(sem["v_res2"], j + 1)
                    gp.dma_start(
                        out=hbm_chunk(out_hbm, j),
                        in_=pb.rearrange("p (g t) -> p g t", t=TC),
                    ).then_inc(sem["gB_out"], 16)
                gp.wait_ge(sem["gB_out"], 16 * NCH)

        sB.close()
    return nc


def kernel(L0_attn_x, L0_attn_Wk, L0_attn_Wr, L0_attn_Wo,
           L0_attn_time_decay, L0_attn_time_first,
           L0_ffn_w_gate, L0_ffn_w_up, L0_ffn_w_down):
    x = np.asarray(L0_attn_x, dtype=np.float32)
    wkT = np.ascontiguousarray(np.asarray(L0_attn_Wk, dtype=np.float32).T)
    wrT = np.ascontiguousarray(np.asarray(L0_attn_Wr, dtype=np.float32).T)
    woT = np.ascontiguousarray(np.asarray(L0_attn_Wo, dtype=np.float32).T)
    td = np.ascontiguousarray(np.asarray(L0_attn_time_decay, dtype=np.float32).reshape(G, D).T)
    tf = np.ascontiguousarray(np.asarray(L0_attn_time_first, dtype=np.float32).reshape(G, D).T)
    wgT = np.ascontiguousarray(np.asarray(L0_ffn_w_gate, dtype=np.int32).T)
    wuT = np.ascontiguousarray(np.asarray(L0_ffn_w_up, dtype=np.int32).T)
    wdT = np.ascontiguousarray(np.asarray(L0_ffn_w_down, dtype=np.int32).T)

    nc = build_nc()
    in_maps = []
    for b in range(NCORE):
        in_maps.append({
            "x_cm": np.ascontiguousarray(x[b].T),
            "wkT": wkT, "wrT": wrT, "woT": woT,
            "td_in": td, "tf_in": tf,
            "wgT": wgT, "wuT": wuT, "wdT": wdT,
        })
    res = run_bass_kernel_spmd(nc, in_maps, core_ids=list(range(NCORE)))
    out = np.stack([np.ascontiguousarray(r["out_cm"].T) for r in res.results])
    return out
